# version 03
# baseline (speedup 1.0000x reference)
import numpy as np
import ml_dtypes
from contextlib import ExitStack

import concourse.bass as bass
import concourse.mybir as mybir
from concourse.bass_utils import run_bass_kernel_spmd

F32 = mybir.dt.float32
BF16 = mybir.dt.bfloat16
AF = mybir.ActivationFunctionType
ALU = mybir.AluOpType
BF16NP = ml_dtypes.bfloat16

S = 4096
D = 1024
KT = 8
NCH = 8
CH = 512
NH = 16
DH = 64
WIN = 256
NBLK = S // WIN
DI = 2730
NGT = (DI + 127) // 128
GSZ = [min(128, DI - g * 128) for g in range(NGT)]
DEPTH_T = 6
LN_EPS = 1e-5

ENGS = ("sync", "tensor", "vector", "scalar", "gpsimd")
SAME_ENGINE_EDGES = True

TAP = "full"
DEBUG_DUMPS = False


def _host_consts():
    c = {}
    pos = np.arange(S)[:, None].astype(np.float64)
    div = np.exp(np.arange(0, 16, 2) * (-np.log(10000.0) / 16))
    pe = np.zeros((S, 16))
    pe[:, 0::2] = np.sin(pos * div)
    pe[:, 1::2] = np.cos(pos * div)
    c["c_pe"] = np.ascontiguousarray((pe / 100.0).T).astype(np.float32)
    half = np.arange(0, DH, 2, dtype=np.float64)
    inv_freq = 1.0 / 10000.0 ** (half / DH)
    t = np.arange(S, dtype=np.float64)
    fr = t[:, None] * inv_freq[None, :]
    base = (half + 0.4 * DH) / (1.4 * DH)
    power = (t - S // 2) / (WIN // 2)
    sc = base[None, :] ** power[:, None]
    cs_q = np.ascontiguousarray((np.cos(fr) * sc).T)
    ss_q = np.ascontiguousarray((np.sin(fr) * sc).T)
    cs_k = np.ascontiguousarray((np.cos(fr) / sc).T)
    ss_k = np.ascontiguousarray((np.sin(fr) / sc).T)
    c["c_csq"] = np.concatenate([cs_q] * 4, 0).astype(np.float32)
    c["c_ssq"] = np.concatenate([ss_q, ss_q, -ss_q, -ss_q], 0).astype(np.float32)
    c["c_csk"] = np.concatenate([cs_k] * 4, 0).astype(np.float32)
    c["c_ssk"] = np.concatenate([ss_k, ss_k, -ss_k, -ss_k], 0).astype(np.float32)
    c["c_id_bf"] = np.eye(128).astype(BF16NP)
    c["c_id_f32"] = np.eye(128).astype(np.float32)
    c["c_id2"] = np.eye(2).astype(np.float32)
    c["c_ones_col"] = np.ones((128, 1)).astype(BF16NP)
    c["c_ones_row1f"] = np.ones((1, 128)).astype(np.float32)
    bd = np.zeros((128, 2))
    bd[0:32, 0] = 1; bd[64:96, 0] = 1
    bd[32:64, 1] = 1; bd[96:128, 1] = 1
    c["c_bd2"] = bd.astype(BF16NP)
    oh = np.zeros((2, 128))
    oh[0, 0:32] = 1; oh[0, 64:96] = 1
    oh[1, 32:64] = 1; oh[1, 96:128] = 1
    c["c_oh2"] = oh.astype(np.float32)
    li = np.arange(128)[:, None]
    cc = np.arange(512)[None, :]
    c["c_mask_p0"] = ((cc >= li) & (cc < li + 257)).astype(BF16NP)
    c["c_mask_p1"] = ((cc >= li + 128) & (cc < li + 128 + 257)).astype(BF16NP)
    c["c_mask_b0_p0"] = (cc <= li).astype(BF16NP)
    c["c_mask_b0_p1"] = (cc <= li + 128).astype(BF16NP)
    return c


_SBUF_CONSTS = ("c_id_bf", "c_id_f32", "c_id2", "c_ones_col", "c_ones_row1f", "c_bd2",
                "c_oh2", "c_mask_p0", "c_mask_p1", "c_mask_b0_p0", "c_mask_b0_p1")


def _host_prep(inputs):
    p = {}
    qperm = np.empty(D, np.int64)
    for m in range(8):
        b = 128 * m
        hA, hB = 64 * (2 * m), 64 * (2 * m + 1)
        qperm[b:b + 32] = np.arange(hA, hA + 32)
        qperm[b + 32:b + 64] = np.arange(hB, hB + 32)
        qperm[b + 64:b + 96] = np.arange(hA + 32, hA + 64)
        qperm[b + 96:b + 128] = np.arange(hB + 32, hB + 64)
    w = inputs["L0_ffn_Wqkv"]
    p["wqkv_p"] = np.ascontiguousarray(
        np.concatenate([w[:, :, qperm], w[:, :, 1024 + qperm], w[:, :, 2048:]], axis=2))
    dmap = np.concatenate([np.arange(32), np.arange(32),
                           np.arange(32, 64), np.arange(32, 64)])
    dswap = np.concatenate([dmap[64:], dmap[:64]])
    qs, ks = inputs["L0_ffn_q_scale"], inputs["L0_ffn_k_scale"]
    p["qs_c"] = np.ascontiguousarray(qs[:, dmap].T)
    p["qs_cs"] = np.ascontiguousarray(qs[:, dswap].T)
    p["ks_c"] = np.ascontiguousarray(ks[:, dmap].T)
    p["ks_cs"] = np.ascontiguousarray(ks[:, dswap].T)

    def cols(v, nt):
        out = np.zeros((128, nt), np.float32)
        for k in range(nt):
            seg = v[128 * k:128 * (k + 1)]
            out[:seg.shape[0], k] = seg
        return out

    p["alogit_c"] = cols(inputs["L0_attn_a_logit"], 8)
    p["bx_c"] = cols(inputs["L0_attn_b_x"], 8)
    p["bgx_c"] = cols(inputs["L0_attn_b_gx"], 8)
    p["bga_c"] = cols(inputs["L0_attn_b_ga"], 8)
    p["bdxo"] = np.ascontiguousarray(
        np.stack([inputs["L0_attn_b_o"], inputs["L0_attn_b_dx"]], 0))
    b1 = inputs["L0_ffn_b1"]
    p["b1a_c"] = np.stack([cols(b1[i, :DI], NGT) for i in range(DEPTH_T)], 0)
    p["b1g_c"] = np.stack([cols(b1[i, DI:], NGT) for i in range(DEPTH_T)], 0)
    p["b2_c"] = np.stack([cols(inputs["L0_ffn_b2"][i], 8) for i in range(DEPTH_T)], 0)
    p["lng_r"] = inputs["L0_ffn_ln_g"]
    p["lnb_r"] = inputs["L0_ffn_ln_b"]
    return p


class Buf:
    __slots__ = ("name", "last_w", "readers")

    def __init__(self, name):
        self.name = name
        self.last_w = None
        self.readers = []


class Op:
    __slots__ = ("eng", "fn", "deps", "dma", "need_inc", "cum", "tag", "dsem")

    def __init__(self, eng, fn, deps, dma, tag):
        self.eng = eng
        self.fn = fn
        self.deps = deps
        self.dma = dma
        self.need_inc = False
        self.cum = 0
        self.dsem = -1
        self.tag = tag


N_DMA_SEMS_PER_Q = 16
DMA_QUEUES = ("sync", "gpsimd", "scalar")
N_DMA_SEMS = N_DMA_SEMS_PER_Q * len(DMA_QUEUES)


class Sched:
    def __init__(self):
        self.ops = {e: [] for e in ENGS}
        self.n_ops = 0
        self.dma_qi = {q: 0 for q in DMA_QUEUES}
        self.dma_counts = [0] * N_DMA_SEMS

    def add(self, eng, fn, reads=(), writes=(), after=(), dma=False, tag=""):
        deps = []
        seen = set()

        def _dep(d):
            if d is not None and id(d) not in seen:
                seen.add(id(d))
                deps.append(d)

        for b in reads:
            _dep(b.last_w)
        for b in writes:
            _dep(b.last_w)
            for r in b.readers:
                _dep(r)
        for d in after:
            _dep(d)
        o = Op(eng, fn, deps, dma, tag)
        if dma:
            qx = DMA_QUEUES.index(eng)
            o.dsem = qx * N_DMA_SEMS_PER_Q + self.dma_qi[eng] % N_DMA_SEMS_PER_Q
            self.dma_qi[eng] += 1
            self.dma_counts[o.dsem] += 16
            o.cum = self.dma_counts[o.dsem]
            o.need_inc = True
        self.ops[eng].append(o)
        self.n_ops += 1
        for b in reads:
            if not b.readers or b.readers[-1] is not o:
                b.readers.append(o)
        for b in writes:
            b.last_w = o
            b.readers = []
        return o

    def _will_wait(self, d, eng):
        if d.eng == eng:
            if eng == "tensor":
                return False
            return SAME_ENGINE_EDGES
        return True

    def finalize(self):
        for eng in ENGS:
            for o in self.ops[eng]:
                for d in o.deps:
                    if self._will_wait(d, eng):
                        d.need_inc = True
        for eng in ENGS:
            cum = 0
            for o in self.ops[eng]:
                if o.dma:
                    continue
                if o.need_inc:
                    cum += 1
                    o.cum = cum

    def emit(self, block, sems, dsems):
        self.finalize()
        for eng in ENGS:
            ops = self.ops[eng]
            dec = getattr(block, eng)

            def make_body(eng=eng, ops=ops):
                def body(q):
                    wm = {e: 0 for e in ENGS}
                    dwm = [0] * N_DMA_SEMS
                    for o in ops:
                        for d in o.deps:
                            if d.dma:
                                if d.cum > dwm[d.dsem]:
                                    q.wait_ge(dsems[d.dsem], d.cum)
                                    dwm[d.dsem] = d.cum
                                continue
                            if not self._will_wait(d, eng):
                                continue
                            if d.cum > wm[d.eng]:
                                q.wait_ge(sems[d.eng], d.cum)
                                wm[d.eng] = d.cum
                        inst = o.fn(q)
                        if o.dma:
                            inst.then_inc(dsems[o.dsem], 16)
                        elif o.need_inc:
                            assert inst is not None, f"op {o.tag} needs inc, emitted nothing"
                            inst.then_inc(sems[eng], 1)
                return body

            dec(make_body())


def build_program(tap=TAP):
    nc = bass.Bass("TRN2", target_bir_lowering=False)

    def din(name, shape, dtype=F32):
        return nc.dram_tensor(name, list(shape), dtype, kind="ExternalInput").ap()

    A = {}
    A["x"] = din("x", (S, D))
    A["W_dx"] = din("W_dx", (1040, D))
    A["W_x"] = din("W_x", (1040, D))
    A["W_gx"] = din("W_gx", (D, D))
    A["W_ga"] = din("W_ga", (D, D))
    A["W_o"] = din("W_o", (D, D))
    A["wqkv_p"] = din("wqkv_p", (DEPTH_T, D, 3 * D))
    A["Wo_t"] = din("Wo_t", (DEPTH_T, D, D))
    A["W1"] = din("W1", (DEPTH_T, D, 2 * DI))
    A["W2"] = din("W2", (DEPTH_T, DI, D))
    A["alogit_c"] = din("alogit_c", (128, 8))
    A["bx_c"] = din("bx_c", (128, 8))
    A["bgx_c"] = din("bgx_c", (128, 8))
    A["bga_c"] = din("bga_c", (128, 8))
    A["bdxo"] = din("bdxo", (2, D))
    A["qs_c"] = din("qs_c", (128, DEPTH_T))
    A["qs_cs"] = din("qs_cs", (128, DEPTH_T))
    A["ks_c"] = din("ks_c", (128, DEPTH_T))
    A["ks_cs"] = din("ks_cs", (128, DEPTH_T))
    A["b1a_c"] = din("b1a_c", (DEPTH_T, 128, NGT))
    A["b1g_c"] = din("b1g_c", (DEPTH_T, 128, NGT))
    A["b2_c"] = din("b2_c", (DEPTH_T, 128, 8))
    A["lng_r"] = din("lng_r", (DEPTH_T, D))
    A["lnb_r"] = din("lnb_r", (DEPTH_T, D))
    consts = _host_consts()
    for k, v in consts.items():
        A[k] = din(k, v.shape, BF16 if v.dtype == BF16NP else F32)

    y_out = nc.dram_tensor("y", [S, D], F32, kind="ExternalOutput").ap()

    def scr(name, shape, dtype=F32):
        return nc.dram_tensor(name, list(shape), dtype, kind="Internal").ap()

    H = {}
    H["xB"] = scr("hb_xB", (D, S))
    H["xin2"] = scr("hb_xin2", (D, S))
    H["xh"] = scr("hb_xh", (D, S))
    H["ga"] = scr("hb_ga", (D, S))
    H["gb"] = scr("hb_gb", (D, S))
    H["h"] = scr("hb_h", (D, S), BF16)
    H["q"] = scr("hb_q", (D, S), BF16)
    H["k"] = scr("hb_k", (D, S), BF16)
    H["vT"] = scr("hb_vT", (S, D), BF16)
    H["ao"] = scr("hb_ao", (D, S), BF16)
    H["y1"] = scr("hb_y1", (D, S))

    sch = Sched()
    gridbufs = {}

    def hbuf(name, key):
        k = (name, key)
        if k not in gridbufs:
            gridbufs[k] = Buf(f"{name}:{key}")
        return gridbufs[k]

    def hrow(name, k):
        return [hbuf(name, (k, n)) for n in range(NCH)]

    es = ExitStack()
    with es:
        def sb(name, shape, dtype):
            t = es.enter_context(nc.sbuf_tensor(name, list(shape), dtype))
            return (t, Buf(name))

        def psalloc(name):
            t = es.enter_context(nc.psum_tensor(name, [128, 512], F32))
            return (t, Buf(name))

        rhs = [sb(f"rhs{k}", [128, S], BF16) for k in range(KT)]
        pe_sb = sb("pe_sb", [16, 512], BF16)
        fw = [sb(f"fw{i}", [128, S], F32) for i in range(2)]
        stg = sb("stg", [128, S], BF16)
        tbl = [sb(f"tbl{i}", [128, S // 2], F32) for i in range(2)]
        wsl = [sb(f"wsl{i}", [128, KT * 128], BF16) for i in range(4)]
        sl = [sb(f"sl{i}", [128, 512], BF16) for i in range(24)]
        fsl = [sb(f"fsl{i}", [128, 512], F32) for i in range(9)]
        ninvk, ninvk_b = sb("ninvk", [2, 2048], F32)
        ln_a, ln_a_b = sb("ln_a", [1, 512], F32)
        ln_be, ln_be_b = sb("ln_be", [1, 512], F32)
        ln_mu, ln_mu_b = sb("ln_mu", [1, 512], F32)
        ln_m2, ln_m2_b = sb("ln_m2", [1, 512], F32)
        cols_t, cols_b = sb("colsb", [128, 64], F32)
        bd_sb, bd_sbb = sb("bd_sb", [2, D], BF16)
        onr2, onr2b = sb("onr2", [2, 512], BF16)
        csb = {}
        for k in _SBUF_CONSTS:
            v = consts[k]
            csb[k] = sb("s" + k, v.shape, BF16 if v.dtype == BF16NP else F32)

        psum = [psalloc(f"pbk{i}") for i in range(8)]

        sems = {e: es.enter_context(nc.semaphore("sem_" + e)) for e in ENGS}
        dsems = [es.enter_context(nc.semaphore(f"dsem{i}")) for i in range(N_DMA_SEMS)]

        out_stores = []

        def load(dst, dst_buf, src_ap, eng="sync", tag="load", reads=(), after=(),
                 slow=False):
            return sch.add(eng, lambda q, d=dst, s=src_ap, sl_=slow: q.dma_start(
                out=d, in_=s, allow_slow_non_contiguous=sl_),
                reads=reads, writes=[dst_buf], after=after, dma=True, tag=tag)

        def loadc(dst, dst_buf, src_ap, tag="loadc", reads=(), after=()):
            return sch.add("gpsimd", lambda q, d=dst, s=src_ap: q.dma_start(out=d, in_=s),
                           reads=reads, writes=[dst_buf], after=after, dma=True, tag=tag)

        def store(dst_ap, src, src_buf, hbufs=(), eng="sync", tag="store", out=False):
            op = sch.add(eng, lambda q, d=dst_ap, s=src: q.dma_start(out=d, in_=s),
                         reads=[src_buf], writes=list(hbufs), dma=True, tag=tag)
            if out:
                out_stores.append(op)
            return op

        def mm(pt, pb, lhsT, lreads, rhsap, rreads, start, stop, tag, tile_position=None):
            return sch.add("tensor",
                           lambda q, pt=pt, lhsT=lhsT, rhsap=rhsap, start=start, stop=stop,
                           tp=tile_position:
                           q.matmul(pt, lhsT, rhsap, start=start, stop=stop,
                                    tile_position=tp),
                           reads=list(lreads) + list(rreads), writes=[pb], tag=tag)

        for k in csb:
            load(csb[k][0][:, :], csb[k][1], A[k], tag=f"const:{k}")
        sch.add("vector", lambda q: q.memset(onr2[:, :], 1.0), writes=[onr2b], tag="onr2")
        loadc(bd_sb[0:2, :], bd_sbb, A["bdxo"], tag="bdxo")

        load(cols_t[:, 8:16], cols_b, A["alogit_c"], tag="alogit")
        load(cols_t[:, 16:24], cols_b, A["bx_c"], tag="bx")
        load(cols_t[:, 24:32], cols_b, A["bgx_c"], tag="bgx")
        load(cols_t[:, 32:40], cols_b, A["bga_c"], tag="bga")
        sch.add("scalar", lambda q: q.activation(cols_t[:, 8:16], cols_t[:, 8:16], AF.Sigmoid),
                reads=[cols_b], writes=[cols_b], tag="aexp")
        sch.add("scalar", lambda q: q.activation(cols_t[:, 0:8], cols_t[:, 8:16], AF.Square),
                reads=[cols_b], writes=[cols_b], tag="aexp2")
        sch.add("scalar", lambda q: q.activation(cols_t[:, 0:8], cols_t[:, 0:8], AF.Sqrt,
                                                 bias=1.0, scale=-1.0),
                reads=[cols_b], writes=[cols_b], tag="sq1ma")

        fcol_t, fcol_b = sb("fcolt", [128, 16], F32)
        nvq, nvq_b = sb("nvq", [128, 512], F32)
        ssq_sb, ssq_sb_b = sb("ssq_sb", [2, 512], F32)
        sb_b1a = sb("b1a_sb", [128, NGT], F32)
        sb_b1g = sb("b1g_sb", [128, NGT], F32)
        lgr, lgr_b = sb("lgr", [1, D], BF16)
        lbr, lbr_b = sb("lbr", [1, D], BF16)
        alf, alf_b = sb("alf", [1, 512], BF16)
        nma, nma_b = sb("nma", [1, 512], BF16)
        _fcol_cache = {}

        def fcol(val):
            if val not in _fcol_cache:
                i = len(_fcol_cache)
                assert i < 16
                sch.add("vector", lambda q, i=i, val=val: q.memset(
                    fcol_t[:, i:i + 1], float(val)), writes=[fcol_b], tag=f"fcol{i}")
                _fcol_cache[val] = i
            i = _fcol_cache[val]
            return fcol_t[:, i:i + 1]

        idf, idf_b = csb["c_id_f32"]
        onc, oncb = csb["c_ones_col"]
        on1f, on1fb = csb["c_ones_row1f"]

        for tt in range(S // 128):
            for half in range(2):
                lt, lb = fsl[(2 * tt + half) % 4]
                load(lt[:, :], lb,
                     A["x"][tt * 128:(tt + 1) * 128, half * 512:(half + 1) * 512],
                     tag=f"tin.l{tt}.{half}")
                pt, pb = psum[(2 * tt + half) % 4]
                for c in range(4):
                    sch.add("tensor",
                            lambda q, pt=pt, lt=lt, c=c: q.transpose(
                                pt[:, c * 128:(c + 1) * 128],
                                lt[:, c * 128:(c + 1) * 128], idf[:, :]),
                            reads=[lb, idf_b], writes=[pb], tag=f"tin.t{tt}.{half}.{c}")
                et, eb = fsl[4 + (2 * tt + half) % 4]
                sch.add("vector", lambda q, et=et, pt=pt: q.tensor_copy(et[:, :], pt[:, :]),
                        reads=[pb], writes=[eb], tag=f"tin.e{tt}.{half}")
                for c in range(4):
                    k = 4 * half + c
                    store(H["xB"][k * 128:(k + 1) * 128, tt * 128:(tt + 1) * 128],
                          et[:, c * 128:(c + 1) * 128], eb,
                          hbufs=[hbuf("xB", (k, tt // 4))], tag=f"tin.s{tt}.{k}")

        wq_i = [0]

        def load_wslice(w2d_ap, c0, c1, tag):
            t, b = wsl[wq_i[0] % len(wsl)]
            wq_i[0] += 1
            w = c1 - c0
            ap = w2d_ap[:, c0:c1].rearrange("(kt p) c -> p kt c", p=128)
            sch.add("gpsimd", lambda q, t=t, ap=ap, w=w: q.dma_start(
                out=t[:, 0:KT * w], in_=ap), writes=[b], dma=True, tag=tag)
            return t, b, w

        def phase_g1():
            for k in range(KT):
                loadc(rhs[k][0][:, :], rhs[k][1], H["xB"][k * 128:(k + 1) * 128, :],
                      reads=hrow("xB", k), tag=f"g1.rhs{k}")
            for m in range(KT):
                wt, wb, _ = load_wslice(A["W_x"][0:1024, :], m * 128, (m + 1) * 128, f"g1.w{m}")
                wpe_t, wpe_b = sl[20 + m % 2]
                loadc(wpe_t[0:16, 0:128], wpe_b,
                      A["W_x"][1024:1040, m * 128:(m + 1) * 128], tag=f"g1.wpe{m}")
                for n in range(NCH):
                    pt, pb = psum[n % 4]
                    for k in range(KT):
                        mm(pt[:, :], pb, wt[:, k * 128:(k + 1) * 128], [wb],
                           rhs[k][0][:, n * CH:(n + 1) * CH], [rhs[k][1]],
                           k == 0, False, f"g1.mm{m}.{n}.{k}")
                    loadc(pe_sb[0][:, :], pe_sb[1], A["c_pe"][:, n * CH:(n + 1) * CH],
                          tag=f"g1.pel{m}.{n}")
                    mm(pt[:, :], pb, wpe_t[0:16, 0:128], [wpe_b],
                       pe_sb[0][:, :], [pe_sb[1]],
                       False, True, f"g1.mmpe{m}.{n}")
                    et, eb = fsl[n % 8]
                    sch.add("vector",
                            lambda q, et=et, pt=pt, m=m: q.tensor_scalar(
                                et[:, :], pt[:, :], cols_t[:, 16 + m:17 + m],
                                cols_t[:, 0 + m:1 + m], ALU.add, ALU.mult),
                            reads=[pb, cols_b], writes=[eb], tag=f"g1.ev{m}.{n}")
                    store(H["xh"][m * 128:(m + 1) * 128, n * CH:(n + 1) * CH], et[:, :], eb,
                          hbufs=[hbuf("xh", (m, n))], tag=f"g1.s{m}.{n}")

        def phase_g2():
            for k in range(KT):
                loadc(rhs[k][0][:, :], rhs[k][1], H["xh"][k * 128:(k + 1) * 128, :],
                      reads=hrow("xh", k), tag=f"g2.rhs{k}")
            for m in range(KT):
                wat, wab, _ = load_wslice(A["W_ga"], m * 128, (m + 1) * 128, f"g2.wa{m}")
                wxt, wxb, _ = load_wslice(A["W_gx"], m * 128, (m + 1) * 128, f"g2.wx{m}")
                xf_t, xf_b = fw[m % 2]
                load(xf_t[:, :], xf_b, H["xh"][m * 128:(m + 1) * 128, :],
                     reads=hrow("xh", m), tag=f"g2.xf{m}")
                for n in range(NCH):
                    pa, pab = psum[(2 * n) % 4]
                    px, pxb = psum[(2 * n + 1) % 4]
                    for k in range(KT):
                        mm(pa[:, :], pab, wat[:, k * 128:(k + 1) * 128], [wab],
                           rhs[k][0][:, n * CH:(n + 1) * CH], [rhs[k][1]],
                           k == 0, k == KT - 1, f"g2.ma{m}.{n}.{k}")
                    for k in range(KT):
                        mm(px[:, :], pxb, wxt[:, k * 128:(k + 1) * 128], [wxb],
                           rhs[k][0][:, n * CH:(n + 1) * CH], [rhs[k][1]],
                           k == 0, k == KT - 1, f"g2.mx{m}.{n}.{k}")
                    at, ab = fsl[0 + n % 2]
                    sch.add("scalar",
                            lambda q, at=at, pa=pa, m=m: q.activation(
                                at[:, :], pa[:, :], AF.Sigmoid, bias=cols_t[:, 32 + m:33 + m]),
                            reads=[pab, cols_b], writes=[ab], tag=f"g2.sa{m}.{n}")
                    sch.add("vector",
                            lambda q, at=at, m=m: q.tensor_scalar(
                                at[:, :], at[:, :], cols_t[:, 8 + m:9 + m], None, ALU.mult),
                            reads=[ab, cols_b], writes=[ab], tag=f"g2.am{m}.{n}")
                    store(H["ga"][m * 128:(m + 1) * 128, n * CH:(n + 1) * CH], at[:, :], ab,
                          hbufs=[hbuf("ga", (m, n))], tag=f"g2.sta{m}.{n}")
                    gt_, gb_ = fsl[2 + n % 2]
                    sch.add("scalar",
                            lambda q, gt_=gt_, px=px, m=m: q.activation(
                                gt_[:, :], px[:, :], AF.Sigmoid, bias=cols_t[:, 24 + m:25 + m]),
                            reads=[pxb, cols_b], writes=[gb_], tag=f"g2.sx{m}.{n}")
                    sch.add("vector",
                            lambda q, gt_=gt_, xf_t=xf_t, n=n: q.tensor_tensor(
                                gt_[:, :], gt_[:, :], xf_t[:, n * CH:(n + 1) * CH], ALU.mult),
                            reads=[gb_, xf_b], writes=[gb_], tag=f"g2.xg{m}.{n}")
                    ut, ub = fsl[4 + n % 2]
                    sch.add("scalar",
                            lambda q, ut=ut, at=at: q.activation(ut[:, :], at[:, :], AF.Square),
                            reads=[ab], writes=[ub], tag=f"g2.sq{m}.{n}")
                    sch.add("scalar",
                            lambda q, ut=ut: q.activation(ut[:, :], ut[:, :], AF.Sqrt,
                                                          bias=1.0, scale=-1.0),
                            reads=[ub], writes=[ub], tag=f"g2.sqr{m}.{n}")
                    sch.add("vector",
                            lambda q, gt_=gt_, ut=ut: q.tensor_tensor(
                                gt_[:, :], gt_[:, :], ut[:, :], ALU.mult),
                            reads=[gb_, ub], writes=[gb_], tag=f"g2.b{m}.{n}")
                    store(H["gb"][m * 128:(m + 1) * 128, n * CH:(n + 1) * CH], gt_[:, :], gb_,
                          hbufs=[hbuf("gb", (m, n))], tag=f"g2.stb{m}.{n}")

        def phase_g3():
            for m in range(KT):
                at, ab = fw[0]
                bt, bb = fw[1]
                load(at[:, :], ab, H["ga"][m * 128:(m + 1) * 128, :],
                     reads=hrow("ga", m), tag=f"g3.la{m}")
                load(bt[:, :], bb, H["gb"][m * 128:(m + 1) * 128, :],
                     reads=hrow("gb", m), tag=f"g3.lb{m}")
                ht, hb = stg
                sch.add("vector",
                        lambda q, ht=ht, at=at, bt=bt: q.tensor_tensor_scan(
                            ht[:, :], at[:, :], bt[:, :], 0.0, ALU.mult, ALU.add),
                        reads=[ab, bb], writes=[hb], tag=f"g3.scan{m}")
                store(H["h"][m * 128:(m + 1) * 128, :], ht[:, :], hb,
                      hbufs=hrow("h", m), tag=f"g3.s{m}")

        def phase_g4():
            for k in range(KT):
                loadc(rhs[k][0][:, :], rhs[k][1], H["xB"][k * 128:(k + 1) * 128, :],
                      reads=hrow("xB", k), tag=f"g4.rx{k}")
            st_s, st_sb = psum[6]
            st_q, st_qb = psum[7]
            for n in range(NCH):
                hk = []
                for k in range(KT):
                    t, b = sl[k]
                    load(t[:, 0:CH], b, H["h"][k * 128:(k + 1) * 128, n * CH:(n + 1) * CH],
                         reads=[hbuf("h", (k, n))], tag=f"g4.rh{k}.{n}")
                    hk.append((t, b))
                ytiles = []
                for m in range(KT):
                    wot, wob, _ = load_wslice(A["W_o"], m * 128, (m + 1) * 128, f"g4.wo{m}.{n}")
                    wdt, wdb, _ = load_wslice(A["W_dx"][0:1024, :], m * 128, (m + 1) * 128,
                                              f"g4.wd{m}.{n}")
                    wpe_t, wpe_b = sl[20 + m % 2]
                    loadc(wpe_t[0:16, 0:128], wpe_b,
                          A["W_dx"][1024:1040, m * 128:(m + 1) * 128], tag=f"g4.wpe{m}.{n}")
                    pt, pb = psum[m % 4]
                    for k in range(KT):
                        mm(pt[:, :], pb, wot[:, k * 128:(k + 1) * 128], [wob],
                           hk[k][0][:, 0:CH], [hk[k][1]], k == 0, False, f"g4.mo{m}.{n}.{k}")
                    for k in range(KT):
                        mm(pt[:, :], pb, wdt[:, k * 128:(k + 1) * 128], [wdb],
                           rhs[k][0][:, n * CH:(n + 1) * CH], [rhs[k][1]],
                           False, False, f"g4.md{m}.{n}.{k}")
                    loadc(pe_sb[0][:, :], pe_sb[1], A["c_pe"][:, n * CH:(n + 1) * CH],
                          tag=f"g4.pel{m}.{n}")
                    mm(pt[:, :], pb, wpe_t[0:16, 0:128], [wpe_b],
                       pe_sb[0][:, :], [pe_sb[1]],
                       False, False, f"g4.mpe{m}.{n}")
                    mm(pt[:, :], pb, bd_sb[0:2, m * 128:(m + 1) * 128], [bd_sbb],
                       onr2[0:2, :], [onr2b], False, True, f"g4.mb{m}.{n}")
                    yt, yb = fsl[m]
                    load(yt[:, :], yb, H["xB"][m * 128:(m + 1) * 128, n * CH:(n + 1) * CH],
                         reads=[hbuf("xB", (m, n))], tag=f"g4.lx{m}.{n}")
                    sch.add("vector",
                            lambda q, yt=yt, pt=pt: q.scalar_tensor_tensor(
                                yt[:, :], pt[:, :], 0.5, yt[:, :], ALU.mult, ALU.add),
                            reads=[pb, yb], writes=[yb], tag=f"g4.y{m}.{n}")
                    ytiles.append((yt, yb))
                    if DEBUG_DUMPS:
                        store(H["y1"][m * 128:(m + 1) * 128, n * CH:(n + 1) * CH],
                              yt[:, :], yb, hbufs=[hbuf("y1", (m, n))],
                              eng="gpsimd", tag=f"g4.dy1{m}.{n}")
                    ybf_t, ybf_b = sl[8 + m]
                    sch.add("scalar", lambda q, ybf_t=ybf_t, yt=yt: q.activation(
                        ybf_t[:, 0:CH], yt[:, :], AF.Copy),
                        reads=[yb], writes=[ybf_b], tag=f"g4.ybf{m}.{n}")
                    ysq_t, ysq_b = sl[16 + m % 4]
                    sch.add("scalar", lambda q, ysq_t=ysq_t, yt=yt: q.activation(
                        ysq_t[:, 0:CH], yt[:, :], AF.Square),
                        reads=[yb], writes=[ysq_b], tag=f"g4.ysq{m}.{n}")
                    mm(st_s[0:1, :], st_sb, onc[:, :], [oncb], ybf_t[:, 0:CH], [ybf_b],
                       m == 0, m == KT - 1, f"g4.sts{m}.{n}")
                    mm(st_q[0:1, :], st_qb, onc[:, :], [oncb], ysq_t[:, 0:CH], [ysq_b],
                       m == 0, m == KT - 1, f"g4.stq{m}.{n}")
                sch.add("vector", lambda q: q.tensor_scalar(
                    ln_a[0:1, :], st_q[0:1, :], 1.0 / D, None, ALU.mult),
                    reads=[st_qb], writes=[ln_a_b], tag=f"g4.v{n}")
                if DEBUG_DUMPS and n == 0:
                    dbg_st2 = nc.dram_tensor("dbg_st2", [1, 4 * 512], F32,
                                             kind="ExternalOutput").ap()
                    def snap(i_, tag):
                        s1t, s1b = fsl[8]
                        sch.add("vector", lambda q, s1t=s1t: q.tensor_copy(
                            s1t[0:1, 0:512], ln_a[0:1, 0:512]),
                            reads=[ln_a_b], writes=[s1b], tag=f"snap{i_}")
                        store(dbg_st2[0:1, 512 * i_:512 * (i_ + 1)], s1t[0:1, 0:512],
                              s1b, tag=f"snapst{i_}", out=True)
                    snap(0, "v")
                sch.add("vector", lambda q: q.tensor_scalar(
                    ln_mu[0:1, :], st_s[0:1, :], 1.0 / D, None, ALU.mult),
                    reads=[st_sb], writes=[ln_mu_b], tag=f"g4.mu{n}")
                sch.add("scalar", lambda q: q.activation(
                    ln_m2[0:1, :], ln_mu[0:1, :], AF.Square),
                    reads=[ln_mu_b], writes=[ln_m2_b], tag=f"g4.mu2{n}")
                sch.add("vector", lambda q: q.tensor_tensor(
                    ln_a[0:1, :], ln_a[0:1, :], ln_m2[0:1, :], ALU.subtract),
                    reads=[ln_a_b, ln_m2_b], writes=[ln_a_b], tag=f"g4.var{n}")
                if DEBUG_DUMPS and n == 0:
                    snap(1, "var")
                eps_ap = fcol(LN_EPS)[0:1, :]
                sch.add("scalar", lambda q, eps_ap=eps_ap: q.activation(
                    ln_a[0:1, :], ln_a[0:1, :], AF.Ln, bias=eps_ap),
                    reads=[ln_a_b, fcol_b], writes=[ln_a_b], tag=f"g4.lnv{n}")
                if DEBUG_DUMPS and n == 0:
                    snap(2, "lnv")
                nh_ap = fcol(-0.5)[0:1, :]
                sch.add("scalar", lambda q, nh_ap=nh_ap: q.activation(
                    ln_a[0:1, :], ln_a[0:1, :], AF.Exp, scale=nh_ap),
                    reads=[ln_a_b, fcol_b], writes=[ln_a_b], tag=f"g4.expv{n}")
                if DEBUG_DUMPS and n == 0:
                    snap(3, "expv")
                sch.add("vector", lambda q: q.tensor_tensor(
                    ln_be[0:1, :], ln_mu[0:1, :], ln_a[0:1, :], ALU.mult),
                    reads=[ln_mu_b, ln_a_b], writes=[ln_be_b], tag=f"g4.beta{n}")
                if DEBUG_DUMPS and n == 0:
                    dbgs = nc.dram_tensor("dbg_strips", [1, 6 * 512], F32,
                                          kind="ExternalOutput").ap()
                    s1t, s1b = fsl[8]
                    for i_, (srct, srcb) in enumerate((
                            (st_q, st_qb), (st_s, st_sb), (ln_a, ln_a_b),
                            (ln_be, ln_be_b), (ln_mu, ln_mu_b), (ln_m2, ln_m2_b))):
                        sch.add("vector", lambda q, srct=srct: q.tensor_copy(
                            s1t[0:1, 0:512], srct[0:1, 0:512]),
                            reads=[srcb], writes=[s1b], tag=f"dbg.s{i_}")
                        store(dbgs[0:1, 512 * i_:512 * (i_ + 1)], s1t[0:1, 0:512], s1b,
                              tag=f"dbg.strips{i_}", out=True)
                pa_t, pa_b = psum[4]
                pq_t, pq_b = psum[5]
                mm(pa_t[:, :], pa_b, on1f[0:1, :], [on1fb], ln_a[0:1, :], [ln_a_b],
                   True, True, f"g4.bca{n}")
                mm(pq_t[:, :], pq_b, on1f[0:1, :], [on1fb], ln_be[0:1, :], [ln_be_b],
                   True, True, f"g4.bcb{n}")
                for m in range(KT):
                    yt, yb = ytiles[m]
                    sch.add("vector", lambda q, yt=yt, pa_t=pa_t: q.tensor_tensor(
                        yt[:, :], yt[:, :], pa_t[:, :], ALU.mult),
                        reads=[yb, pa_b], writes=[yb], tag=f"g4.ap1{m}.{n}")
                    sch.add("vector", lambda q, yt=yt, pq_t=pq_t: q.tensor_tensor(
                        yt[:, :], yt[:, :], pq_t[:, :], ALU.subtract),
                        reads=[yb, pq_b], writes=[yb], tag=f"g4.ap2{m}.{n}")
                    store(H["xB"][m * 128:(m + 1) * 128, n * CH:(n + 1) * CH], yt[:, :], yb,
                          hbufs=[hbuf("xB", (m, n))], tag=f"g4.sx{m}.{n}")
                    store(H["xin2"][m * 128:(m + 1) * 128, n * CH:(n + 1) * CH], yt[:, :], yb,
                          hbufs=[hbuf("xin2", (m, n))], eng="gpsimd", tag=f"g4.sx2{m}.{n}")


        viewbufs = {}

        def vb(name):
            if name not in viewbufs:
                viewbufs[name] = Buf(name)
            return viewbufs[name]

        def emit_shift_tiles(dst_tiles, tag):
            for k in range(KT):
                t, b = dst_tiles[k]
                if k < 4:
                    loadc(t[:, :], b, H["xB"][k * 128:(k + 1) * 128, :],
                          reads=hrow("xB", k), tag=f"{tag}.s{k}")
                else:
                    sch.add("vector", lambda q, t=t: q.memset(t[:, 0:1], 0.0),
                            writes=[b], tag=f"{tag}.z{k}")
                    loadc(t[:, 1:S], b, H["xB"][k * 128:(k + 1) * 128, 0:S - 1],
                          reads=hrow("xB", k), tag=f"{tag}.s{k}")

        def phase_qkv(i):
            wq = A["wqkv_p"][i]
            emit_shift_tiles(rhs, f"L{i}.qs")
            load(cols_t[:, 48:49], cols_b, A["qs_c"][:, i:i + 1], tag=f"L{i}.qsc", slow=True)
            load(cols_t[:, 50:51], cols_b, A["ks_c"][:, i:i + 1], tag=f"L{i}.ksc", slow=True)
            HS = S // 2
            idb_t, idb_b = csb["c_id_bf"]
            id2t, id2b = csb["c_id2"]
            bd2t, bd2b = csb["c_bd2"]
            oh2t, oh2b = csb["c_oh2"]
            psT, psT_b = psum[6]
            first_T = [True]
            for side in range(2):
                csname = "c_csq" if side == 0 else "c_csk"
                ssname = "c_ssq" if side == 0 else "c_ssk"
                ccol = 48 + 2 * side
                for half in range(2):
                    t0, t0b = tbl[0]
                    t1, t1b = tbl[1]
                    load(t0[:, :], t0b, A[csname][:, half * HS:(half + 1) * HS],
                         tag=f"L{i}.tc{side}.{half}")
                    load(t1[:, :], t1b, A[ssname][:, half * HS:(half + 1) * HS],
                         tag=f"L{i}.ts{side}.{half}")
                    sch.add("vector", lambda q, t0=t0, ccol=ccol: q.tensor_scalar(
                        t0[:, :], t0[:, :], cols_t[:, ccol:ccol + 1], None, ALU.mult),
                        reads=[t0b, cols_b], writes=[t0b], tag=f"L{i}.tsc{side}.{half}")
                    sch.add("vector", lambda q, t1=t1, ccol=ccol: q.tensor_scalar(
                        t1[:, :], t1[:, :], cols_t[:, ccol:ccol + 1], None, ALU.mult),
                        reads=[t1b, cols_b], writes=[t1b], tag=f"L{i}.tss{side}.{half}")
                    for m in range(8):
                        wt, wb, _ = load_wslice(wq, 1024 * side + 128 * m,
                                                1024 * side + 128 * (m + 1),
                                                f"L{i}.w{side}.{half}.{m}")
                        qf = fw[0]
                        for nn in range(4):
                            pt, pb = psum[nn % 4]
                            for k in range(KT):
                                mm(pt[:, :], pb, wt[:, k * 128:(k + 1) * 128], [wb],
                                   rhs[k][0][:, half * HS + nn * CH:half * HS + (nn + 1) * CH],
                                   [rhs[k][1]], k == 0, k == KT - 1,
                                   f"L{i}.qkm{side}.{half}.{m}.{nn}")
                            sch.add("scalar", lambda q, pt=pt, nn=nn: q.activation(
                                qf[0][:, nn * CH:(nn + 1) * CH], pt[:, :], AF.Copy),
                                reads=[pb], writes=[qf[1]], tag=f"L{i}.qe{side}.{half}.{m}.{nn}")
                        q2v = sl[0]
                        for nn in range(4):
                            sch.add("scalar", lambda q, q2v=q2v, nn=nn: q.activation(
                                q2v[0][:, 0:CH], qf[0][:, nn * CH:(nn + 1) * CH], AF.Square),
                                reads=[qf[1]], writes=[q2v[1]], tag=f"L{i}.q2{side}.{half}.{m}.{nn}")
                            pss, pssb = psum[5]
                            mm(pss[0:2, :], pssb, bd2t[:, :], [bd2b], q2v[0][:, 0:CH],
                               [q2v[1]], True, True, f"L{i}.ssm{side}.{half}.{m}.{nn}")
                            if side == 0:
                                sch.add("vector", lambda q, pss=pss: q.tensor_copy(
                                    ssq_sb[0:2, :], pss[0:2, :]),
                                    reads=[pssb], writes=[ssq_sb_b],
                                    tag=f"L{i}.ssc{half}.{m}.{nn}")
                                for c in range(4):
                                    cg = 16 * half + 4 * nn + c
                                    last_T = (half == 1 and m == 7 and nn == 3 and c == 3)
                                    mm(psT[:, cg * 16 + 2 * m:cg * 16 + 2 * m + 2], psT_b,
                                       ssq_sb[0:2, 128 * c:128 * (c + 1)], [ssq_sb_b],
                                       id2t[:, :], [id2b], first_T[0], last_T,
                                       f"L{i}.ssT{half}.{m}.{nn}.{c}")
                                    first_T[0] = False
                            else:
                                sch.add("vector", lambda q, pss=pss, nn=nn: q.tensor_scalar(
                                    ninvk[0:2, nn * CH:(nn + 1) * CH], pss[0:2, :],
                                    1e-24, None, ALU.max),
                                    reads=[pssb], writes=[ninvk_b],
                                    tag=f"L{i}.skc{half}.{m}.{nn}")
                        if side == 1:
                            nh2 = fcol(-0.5)[0:2, :]
                            sch.add("scalar", lambda q: q.activation(
                                ninvk[0:2, :], ninvk[0:2, :], AF.Ln),
                                reads=[ninvk_b], writes=[ninvk_b], tag=f"L{i}.kln{half}.{m}")
                            sch.add("scalar", lambda q, nh2=nh2: q.activation(
                                ninvk[0:2, :], ninvk[0:2, :], AF.Exp, scale=nh2),
                                reads=[ninvk_b, fcol_b], writes=[ninvk_b],
                                tag=f"L{i}.kex{half}.{m}")
                        u = fw[1]
                        sch.add("vector", lambda q, u=u, t0=t0: q.tensor_tensor(
                            u[0][:, 0:HS], qf[0][:, 0:HS], t0[:, :], ALU.mult),
                            reads=[qf[1], t0b], writes=[u[1]], tag=f"L{i}.r1{side}.{half}.{m}")
                        sch.add("vector", lambda q, u=u, t1=t1: q.tensor_tensor(
                            u[0][0:64, HS:2 * HS], qf[0][64:128, 0:HS],
                            t1[64:128, :], ALU.mult),
                            reads=[qf[1], t1b], writes=[u[1]], tag=f"L{i}.r2{side}.{half}.{m}")
                        sch.add("vector", lambda q, u=u, t1=t1: q.tensor_tensor(
                            u[0][64:128, HS:2 * HS], qf[0][0:64, 0:HS],
                            t1[0:64, :], ALU.mult),
                            reads=[qf[1], t1b], writes=[u[1]], tag=f"L{i}.r3{side}.{half}.{m}")
                        sch.add("vector", lambda q, u=u: q.tensor_tensor(
                            u[0][:, 0:HS], u[0][:, 0:HS], u[0][:, HS:2 * HS], ALU.add),
                            reads=[u[1]], writes=[u[1]], tag=f"L{i}.r4{side}.{half}.{m}")
                        if side == 1:
                            for nn in range(4):
                                pnb, pnbb = psum[4]
                                mm(pnb[:, :], pnbb, oh2t[:, :], [oh2b],
                                   ninvk[0:2, nn * CH:(nn + 1) * CH], [ninvk_b],
                                   True, True, f"L{i}.nbm{half}.{m}.{nn}")
                                sch.add("vector", lambda q, u=u, pnb=pnb, nn=nn: q.tensor_tensor(
                                    u[0][:, nn * CH:(nn + 1) * CH],
                                    u[0][:, nn * CH:(nn + 1) * CH], pnb[:, :], ALU.mult),
                                    reads=[u[1], pnbb], writes=[u[1]],
                                    tag=f"L{i}.nba{half}.{m}.{nn}")
                        qbf = stg
                        sch.add("scalar", lambda q, u=u: q.activation(
                            qbf[0][:, 0:HS], u[0][:, 0:HS], AF.Copy),
                            reads=[u[1]], writes=[qbf[1]], tag=f"L{i}.qb{side}.{half}.{m}")
                        dsttn = H["q"] if side == 0 else H["k"]
                        dname = "q" if side == 0 else "k"
                        for blkid, (pr, dr) in enumerate((
                                (0, 0), (32, 64), (64, 32), (96, 96))):
                            store(dsttn[128 * m + dr:128 * m + dr + 32,
                                        half * HS:(half + 1) * HS],
                                  qbf[0][pr:pr + 32, 0:HS], qbf[1],
                                  hbufs=[vb(f"{dname}{m}.{half}.{blkid}")],
                                  tag=f"L{i}.st{side}.{half}.{m}.{blkid}")
            sch.add("vector", lambda q: q.tensor_copy(nvq[:, :], psT[:, :]),
                    reads=[psT_b], writes=[nvq_b], tag=f"L{i}.nvc")
            sch.add("vector", lambda q: q.tensor_scalar(
                nvq[:, :], nvq[:, :], 1e-24, None, ALU.max),
                reads=[nvq_b], writes=[nvq_b], tag=f"L{i}.nvm")
            l8 = fcol(float(np.log(8.0)))
            nh = fcol(-0.5)
            sch.add("scalar", lambda q: q.activation(nvq[:, :], nvq[:, :], AF.Ln),
                    reads=[nvq_b], writes=[nvq_b], tag=f"L{i}.nvl")
            sch.add("scalar", lambda q, l8=l8, nh=nh: q.activation(
                nvq[:, :], nvq[:, :], AF.Exp, scale=nh, bias=l8),
                reads=[nvq_b, fcol_b], writes=[nvq_b], tag=f"L{i}.nve")
            for half in range(2):
                for m in range(8):
                    wt, wb, _ = load_wslice(wq, 2048 + 128 * m, 2048 + 128 * (m + 1),
                                            f"L{i}.wv{half}.{m}")
                    vbf = stg
                    for nn in range(4):
                        pt, pb = psum[nn % 4]
                        for k in range(KT):
                            mm(pt[:, :], pb, wt[:, k * 128:(k + 1) * 128], [wb],
                               rhs[k][0][:, half * HS + nn * CH:half * HS + (nn + 1) * CH],
                               [rhs[k][1]], k == 0, k == KT - 1, f"L{i}.vm{half}.{m}.{nn}")
                        sch.add("scalar", lambda q, pt=pt, nn=nn: q.activation(
                            vbf[0][:, nn * CH:(nn + 1) * CH], pt[:, :], AF.Copy),
                            reads=[pb], writes=[vbf[1]], tag=f"L{i}.ve{half}.{m}.{nn}")
                    for grp in range(4):
                        ptv, ptvb = psum[4 + grp % 2]
                        for c4 in range(4):
                            c = 4 * grp + c4
                            sch.add("tensor", lambda q, ptv=ptv, c=c, c4=c4: q.matmul(
                                ptv[:, c4 * 128:(c4 + 1) * 128],
                                vbf[0][:, c * 128:(c + 1) * 128], idb_t[:, :],
                                start=True, stop=True),
                                reads=[vbf[1], idb_b], writes=[ptvb],
                                tag=f"L{i}.vT{half}.{m}.{c}")
                        ev = sl[1 + grp % 2]
                        sch.add("vector", lambda q, ev=ev, ptv=ptv: q.tensor_copy(
                            ev[0][:, 0:CH], ptv[:, :]),
                            reads=[ptvb], writes=[ev[1]], tag=f"L{i}.vTe{half}.{m}.{grp}")
                        for c4 in range(4):
                            c = 4 * grp + c4
                            tok0 = half * HS + 128 * c
                            store(H["vT"][tok0:tok0 + 128, 128 * m:128 * (m + 1)],
                                  ev[0][:, c4 * 128:(c4 + 1) * 128], ev[1],
                                  hbufs=[vb(f"vT{m}.{half}.{c}")],
                                  tag=f"L{i}.vTs{half}.{m}.{c}")

        def phase_attn(i):
            idb_t, idb_b = csb["c_id_bf"]
            for blk in range(NBLK):
                w0 = 0 if blk == 0 else 256 * blk - 256
                mp0 = csb["c_mask_b0_p0" if blk == 0 else "c_mask_p0"]
                mp1 = csb["c_mask_b0_p1" if blk == 0 else "c_mask_p1"]
                first_blk = blk == 0
                qbv = []
                for p in range(8):
                    t = stg[0][:, 256 * p:256 * (p + 1)]
                    b = vb(f"at.qb{p}")
                    wr = [b] + ([stg[1]] if (first_blk and p == 0) else [])
                    rd = ([] if (first_blk and p == 0) else [stg[1]]) + \
                         [vb(f"q{p}.{hh}.{bb}") for hh in range(2) for bb in range(4)]
                    sch.add("sync", lambda q, t=t, s=H["q"][128 * p:128 * (p + 1),
                                                            256 * blk:256 * (blk + 1)]:
                            q.dma_start(out=t, in_=s),
                            reads=rd, writes=wr, dma=True, tag=f"L{i}.aq{blk}.{p}")
                    qbv.append((t, b))
                kwv = []
                for p in range(8):
                    t = rhs[0][0][:, 512 * p:512 * (p + 1)]
                    b = vb(f"at.kw{p}")
                    wr = [b] + ([rhs[0][1]] if (first_blk and p == 0) else [])
                    rd = ([] if (first_blk and p == 0) else [rhs[0][1]]) + \
                         [vb(f"k{p}.{hh}.{bb}") for hh in range(2) for bb in range(4)]
                    sch.add("sync", lambda q, t=t, s=H["k"][128 * p:128 * (p + 1),
                                                            w0:w0 + 512]:
                            q.dma_start(out=t, in_=s),
                            reads=rd, writes=wr, dma=True, tag=f"L{i}.ak{blk}.{p}")
                    kwv.append((t, b))
                vwv = []
                for rt in range(4):
                    for cc in range(2):
                        t = rhs[1][0][:, 512 * (2 * rt + cc):512 * (2 * rt + cc + 1)]
                        b = vb(f"at.vw{rt}.{cc}")
                        frst = first_blk and rt == 0 and cc == 0
                        wr = [b] + ([rhs[1][1]] if frst else [])
                        rd = ([] if frst else [rhs[1][1]]) + \
                             [vb(f"vT{m}.{hh}.{bb}") for m in range(8)
                              for hh in range(2) for bb in range(32)]
                        sch.add("sync", lambda q, t=t,
                                s=H["vT"][w0 + 128 * rt:w0 + 128 * (rt + 1),
                                          512 * cc:512 * (cc + 1)]:
                                q.dma_start(out=t, in_=s),
                                reads=rd, writes=wr, dma=True, tag=f"L{i}.av{blk}.{rt}.{cc}")
                        vwv.append((t, b))
                for pr in range(8):
                    if blk == 0:
                        acs_it = ([0], [0, 1])
                    else:
                        acs_it = ([0, 1, 2], [1, 2, 3])
                    pT_views = []
                    for hoff in range(2):
                        h = 2 * pr + hoff
                        for it in range(2):
                            acs = acs_it[it]
                            c0, c1 = 128 * acs[0], 128 * (acs[-1] + 1)
                            W = c1 - c0
                            cg = 2 * blk + it
                            ps_s, ps_sb = psum[it]
                            mm(ps_s[:, 0:W], ps_sb,
                               qbv[pr][0][64 * hoff:64 * (hoff + 1), it * 128:(it + 1) * 128],
                               [qbv[pr][1]],
                               kwv[pr][0][64 * hoff:64 * (hoff + 1), c0:c1], [kwv[pr][1]],
                               True, True, f"L{i}.sc{blk}.{h}.{it}")
                            pmv = rhs[2][0][:, 512 * (2 * hoff + it):512 * (2 * hoff + it) + 512]
                            pmb = vb(f"at.pm{2 * hoff + it}")
                            frst = first_blk and pr == 0 and hoff == 0 and it == 0
                            sch.add("scalar", lambda q, pmv=pmv, ps_s=ps_s, cg=cg, h=h, W=W:
                                    q.activation(pmv[:, 0:W], ps_s[:, 0:W], AF.Exp,
                                                 scale=nvq[:, cg * 16 + h:cg * 16 + h + 1]),
                                    reads=[ps_sb, nvq_b] + ([] if frst else [rhs[2][1]]),
                                    writes=[pmb] + ([rhs[2][1]] if frst else []),
                                    tag=f"L{i}.ex{blk}.{h}.{it}")
                            mk = mp0 if it == 0 else mp1
                            rcol = fcol_t[:, 8 + 2 * hoff + it:9 + 2 * hoff + it]
                            sch.add("vector", lambda q, pmv=pmv, mk=mk, rcol=rcol, c0=c0, c1=c1, W=W:
                                    q.scalar_tensor_tensor(
                                        pmv[:, 0:W], pmv[:, 0:W], 1.0, mk[0][:, c0:c1],
                                        ALU.mult, ALU.mult, accum_out=rcol),
                                    reads=[pmb, mk[1]], writes=[pmb, fcol_b],
                                    tag=f"L{i}.mk{blk}.{h}.{it}")
                            sch.add("vector", lambda q, rcol=rcol: q.reciprocal(
                                rcol, rcol),
                                reads=[fcol_b], writes=[fcol_b],
                                tag=f"L{i}.rc{blk}.{h}.{it}")
                            sch.add("vector", lambda q, pmv=pmv, rcol=rcol, W=W: q.tensor_scalar(
                                pmv[:, 0:W], pmv[:, 0:W], rcol, None, ALU.mult),
                                reads=[pmb, fcol_b], writes=[pmb],
                                tag=f"L{i}.nm{blk}.{h}.{it}")
                            for ji, jc in enumerate(acs):
                                ptT, ptTb = psum[2 + jc // 2]
                                mm(ptT[:, 256 * (jc % 2) + 128 * it:
                                       256 * (jc % 2) + 128 * (it + 1)], ptTb,
                                   pmv[:, 128 * ji:128 * (ji + 1)], [pmb],
                                   idb_t[:, :], [idb_b], True, True,
                                   f"L{i}.pT{blk}.{h}.{it}.{jc}")
                        for bk in range(2):
                            ptT, ptTb = psum[2 + bk]
                            t = rhs[3][0][:, 512 * (2 * hoff + bk):512 * (2 * hoff + bk) + 512]
                            b = vb(f"at.pT{2 * hoff + bk}")
                            frst = first_blk and pr == 0 and hoff == 0 and bk == 0
                            sch.add("vector", lambda q, t=t, ptT=ptT: q.tensor_copy(
                                t, ptT[:, :]),
                                reads=[ptTb] + ([] if frst else [rhs[3][1]]),
                                writes=[b] + ([rhs[3][1]] if frst else []),
                                tag=f"L{i}.pTe{blk}.{h}.{bk}")
                            pT_views.append((t, b))
                    aov = rhs[4][0][:, 256 * (pr % 4):256 * (pr % 4) + 256]
                    aob = vb(f"at.ao{pr % 4}")
                    for hoff in range(2):
                        h = 2 * pr + hoff
                        po, pob = psum[4 + hoff]
                        vcc = (64 * h) // 512
                        vco = (64 * h) % 512
                        mm_list = []
                        for jc in range(4):
                            its = [it for it in range(2) if jc in acs_it[it]]
                            if not its:
                                continue
                            lo = 128 * its[0]
                            hi = 128 * (its[-1] + 1)
                            pv = pT_views[2 * hoff + jc // 2]
                            rhs_ap = pv[0][:, 256 * (jc % 2) + lo:256 * (jc % 2) + hi]
                            vw = vwv[2 * jc + vcc]
                            mm_list.append((jc, lo, hi, pv, rhs_ap, vw))
                        for ix, (jc, lo, hi, pv, rhs_ap, vw) in enumerate(mm_list):
                            mm(po[0:64, lo:hi], pob,
                               vw[0][:, vco:vco + 64], [vw[1]], rhs_ap, [pv[1]],
                               ix == 0, ix == len(mm_list) - 1, f"L{i}.ov{blk}.{h}.{jc}")
                        frst0 = first_blk and pr == 0 and hoff == 0
                        sch.add("vector", lambda q, aov=aov, po=po, hoff=hoff: q.tensor_copy(
                            aov[64 * hoff:64 * (hoff + 1), :], po[0:64, 0:256]),
                            reads=[pob] + ([] if frst0 else [rhs[4][1]]),
                            writes=[aob] + ([rhs[4][1]] if frst0 else []),
                            tag=f"L{i}.aoe{blk}.{pr}.{hoff}")
                    if DEBUG_DUMPS and blk == 2 and pr == 0 and i == 0:
                        dbg_pm = nc.dram_tensor("dbg_pm", [128, 4 * 512], BF16,
                                                kind="ExternalOutput").ap()
                        dbg_pt = nc.dram_tensor("dbg_pt", [128, 4 * 512], BF16,
                                                kind="ExternalOutput").ap()
                        for sl_ in range(4):
                            store(dbg_pm[:, 512 * sl_:512 * (sl_ + 1)],
                                  rhs[2][0][:, 512 * sl_:512 * (sl_ + 1)],
                                  vb(f"at.pm{sl_}"), tag=f"dbgpm{sl_}", out=True)
                            store(dbg_pt[:, 512 * sl_:512 * (sl_ + 1)],
                                  rhs[3][0][:, 512 * sl_:512 * (sl_ + 1)],
                                  vb(f"at.pT{sl_}"), tag=f"dbgpt{sl_}", out=True)
                    aov = rhs[4][0][:, 256 * (pr % 4):256 * (pr % 4) + 256]
                    aob = vb(f"at.ao{pr % 4}")
                    store(H["ao"][128 * pr:128 * (pr + 1), 256 * blk:256 * (blk + 1)],
                          aov, aob, hbufs=[hbuf("ao", (pr, blk // 2))],
                          tag=f"L{i}.aos{blk}.{pr}")

        def phase_wo(i):
            wo = A["Wo_t"][i]
            for n in range(NCH):
                for k in range(KT):
                    t, b = sl[k]
                    load(t[:, 0:CH], b, H["ao"][k * 128:(k + 1) * 128, n * CH:(n + 1) * CH],
                         reads=[hbuf("ao", (k, n))], tag=f"L{i}.wo.l{k}.{n}")
                for m in range(KT):
                    wt, wb, _ = load_wslice(wo, 128 * m, 128 * (m + 1), f"L{i}.wo.w{m}.{n}")
                    pt, pb = psum[m % 4]
                    for k in range(KT):
                        mm(pt[:, :], pb, wt[:, k * 128:(k + 1) * 128], [wb],
                           sl[k][0][:, 0:CH], [sl[k][1]], k == 0, k == KT - 1,
                           f"L{i}.wo.m{m}.{n}.{k}")
                    yt, yb = fsl[m % 4]
                    load(yt[:, :], yb, H["xB"][m * 128:(m + 1) * 128, n * CH:(n + 1) * CH],
                         reads=[hbuf("xB", (m, n))], tag=f"L{i}.wo.x{m}.{n}")
                    sch.add("vector", lambda q, yt=yt, pt=pt: q.tensor_tensor(
                        yt[:, :], pt[:, :], yt[:, :], ALU.add),
                        reads=[pb, yb], writes=[yb], tag=f"L{i}.wo.y{m}.{n}")
                    store(H["xB"][m * 128:(m + 1) * 128, n * CH:(n + 1) * CH], yt[:, :], yb,
                          hbufs=[hbuf("xB", (m, n))], tag=f"L{i}.wo.s{m}.{n}")

        def load_w2slice(w2, m, tag):
            outs = []
            e1 = wsl[wq_i[0] % len(wsl)]; wq_i[0] += 1
            e2 = wsl[wq_i[0] % len(wsl)]; wq_i[0] += 1
            e3 = wsl[wq_i[0] % len(wsl)]; wq_i[0] += 1
            c0, c1 = 128 * m, 128 * (m + 1)
            ap1 = w2[0:1024, c0:c1].rearrange("(kt p) c -> p kt c", p=128)
            sch.add("gpsimd", lambda q, t=e1[0], ap=ap1: q.dma_start(
                out=t[:, 0:1024], in_=ap), writes=[e1[1]], dma=True, tag=tag + "a")
            ap2 = w2[1024:2048, c0:c1].rearrange("(kt p) c -> p kt c", p=128)
            sch.add("gpsimd", lambda q, t=e2[0], ap=ap2: q.dma_start(
                out=t[:, 0:1024], in_=ap), writes=[e2[1]], dma=True, tag=tag + "b")
            ap3 = w2[2048:2688, c0:c1].rearrange("(kt p) c -> p kt c", p=128)
            sch.add("gpsimd", lambda q, t=e3[0], ap=ap3: q.dma_start(
                out=t[:, 0:640], in_=ap), writes=[e3[1]], dma=True, tag=tag + "c")
            sch.add("gpsimd", lambda q, t=e3[0], ap=w2[2688:2730, c0:c1]: q.dma_start(
                out=t[0:42, 640:768], in_=ap), writes=[e3[1]], dma=True, tag=tag + "d")
            for g in range(NGT):
                if g < 8:
                    outs.append((e1[0][:, 128 * g:128 * (g + 1)], e1[1], 128))
                elif g < 16:
                    outs.append((e2[0][:, 128 * (g - 8):128 * (g - 7)], e2[1], 128))
                elif g < 21:
                    outs.append((e3[0][:, 128 * (g - 16):128 * (g - 15)], e3[1], 128))
                else:
                    outs.append((e3[0][0:42, 640:768], e3[1], 42))
            return outs

        def phase_ffn(i):
            w1 = A["W1"][i]
            w2 = A["W2"][i]
            loadc(lgr[0:1, :], lgr_b, A["lng_r"][i:i + 1, :], tag=f"L{i}.lgr")
            loadc(lbr[0:1, :], lbr_b, A["lnb_r"][i:i + 1, :], tag=f"L{i}.lbr")
            load(cols_t[:, 40:48], cols_b,
                 A["b2_c"][i].rearrange("p k -> p k") if False else A["b2_c"][i],
                 tag=f"L{i}.b2c")
            b1a_t, b1a_b = sb_b1a
            b1g_t, b1g_b = sb_b1g
            load(b1a_t[:, :], b1a_b, A["b1a_c"][i], tag=f"L{i}.b1a")
            load(b1g_t[:, :], b1g_b, A["b1g_c"][i], tag=f"L{i}.b1g")
            onc, oncb = csb["c_ones_col"]
            st_s, st_sb = psum[6]
            st_q, st_qb = psum[7]
            for bi in range(3):
                tb = 1024 * (bi + 1) - 1
                for k in range(4, 8):
                    load(cols_t[:, 52 + 3 * (k - 4) + bi:53 + 3 * (k - 4) + bi], cols_b,
                         H["xB"][k * 128:(k + 1) * 128, tb:tb + 1],
                         reads=[hbuf("xB", (k, tb // 512))], slow=True,
                         tag=f"L{i}.f.bnd{bi}.{k}")
            for npair in range(4):
                xn_tiles = {}
                for sub in range(2):
                    n = 2 * npair + sub
                    s2s = []
                    for k in range(KT):
                        t, b = sl[k] if sub == 0 else sl[8 + k]
                        if k < 4:
                            loadc(t[:, 0:CH], b,
                                  H["xB"][k * 128:(k + 1) * 128, n * CH:(n + 1) * CH],
                                  reads=[hbuf("xB", (k, n))], tag=f"L{i}.f.s{k}.{n}")
                        elif n == 0:
                            sch.add("vector", lambda q, t=t: q.memset(t[:, 0:1], 0.0),
                                    writes=[b], tag=f"L{i}.f.z{k}")
                            loadc(t[:, 1:CH], b,
                                  H["xB"][k * 128:(k + 1) * 128, 0:CH - 1],
                                  reads=[hbuf("xB", (k, 0))], tag=f"L{i}.f.s{k}.{n}")
                        elif sub == 0:
                            cc_ = 52 + 3 * (k - 4) + (npair - 1)
                            sch.add("vector", lambda q, t=t, cc_=cc_: q.tensor_copy(
                                t[:, 0:1], cols_t[:, cc_:cc_ + 1]),
                                reads=[cols_b], writes=[b], tag=f"L{i}.f.bc{k}.{n}")
                            loadc(t[:, 1:CH], b,
                                  H["xB"][k * 128:(k + 1) * 128, n * CH:(n + 1) * CH - 1],
                                  reads=[hbuf("xB", (k, n))], tag=f"L{i}.f.s{k}.{n}")
                        else:
                            loadc(t[:, 0:CH], b,
                                  H["xB"][k * 128:(k + 1) * 128, n * CH - 1:(n + 1) * CH - 1],
                                  reads=[hbuf("xB", (k, n - 1)), hbuf("xB", (k, n))],
                                  tag=f"L{i}.f.s{k}.{n}")
                        s2s.append((t, b))
                    for k in range(KT):
                        sq = rhs[7][0][:, 512 * (k % 2):512 * (k % 2) + 512]
                        sqb = vb(f"ffn.sq{k % 2}")
                        frst = npair == 0 and sub == 0 and k == 0
                        sch.add("scalar", lambda q, sq=sq, t=s2s[k][0]: q.activation(
                            sq, t[:, 0:CH], AF.Square),
                            reads=[s2s[k][1]] + ([] if frst else [rhs[7][1]]),
                            writes=[sqb] + ([rhs[7][1]] if frst else []),
                            tag=f"L{i}.f.q{k}.{n}")
                        mm(st_s[0:1, :], st_sb, onc[:, :], [oncb], s2s[k][0][:, 0:CH],
                           [s2s[k][1]], k == 0, k == KT - 1, f"L{i}.f.sts{k}.{n}")
                        mm(st_q[0:1, :], st_qb, onc[:, :], [oncb], sq, [sqb],
                           k == 0, k == KT - 1, f"L{i}.f.stq{k}.{n}")
                    sch.add("vector", lambda q: q.tensor_scalar(
                        ln_a[0:1, :], st_q[0:1, :], 1.0 / D, None, ALU.mult),
                        reads=[st_qb], writes=[ln_a_b], tag=f"L{i}.f.v{n}")
                    sch.add("vector", lambda q: q.tensor_scalar(
                        ln_mu[0:1, :], st_s[0:1, :], 1.0 / D, None, ALU.mult),
                        reads=[st_sb], writes=[ln_mu_b], tag=f"L{i}.f.mu{n}")
                    sch.add("scalar", lambda q: q.activation(
                        ln_m2[0:1, :], ln_mu[0:1, :], AF.Square),
                        reads=[ln_mu_b], writes=[ln_m2_b], tag=f"L{i}.f.m2{n}")
                    sch.add("vector", lambda q: q.tensor_tensor(
                        ln_a[0:1, :], ln_a[0:1, :], ln_m2[0:1, :], ALU.subtract),
                        reads=[ln_a_b, ln_m2_b], writes=[ln_a_b], tag=f"L{i}.f.var{n}")
                    eps_ap = fcol(LN_EPS)[0:1, :]
                    sch.add("scalar", lambda q, eps_ap=eps_ap: q.activation(
                        ln_a[0:1, :], ln_a[0:1, :], AF.Ln, bias=eps_ap),
                        reads=[ln_a_b, fcol_b], writes=[ln_a_b], tag=f"L{i}.f.ln{n}")
                    nh_ap = fcol(-0.5)[0:1, :]
                    sch.add("scalar", lambda q, nh_ap=nh_ap: q.activation(
                        ln_a[0:1, :], ln_a[0:1, :], AF.Exp, scale=nh_ap),
                        reads=[ln_a_b, fcol_b], writes=[ln_a_b], tag=f"L{i}.f.ex{n}")
                    sch.add("vector", lambda q: q.tensor_tensor(
                        ln_be[0:1, :], ln_mu[0:1, :], ln_a[0:1, :], ALU.mult),
                        reads=[ln_mu_b, ln_a_b], writes=[ln_be_b], tag=f"L{i}.f.be{n}")
                    sch.add("vector", lambda q: q.tensor_copy(alf[0:1, :], ln_a[0:1, :]),
                            reads=[ln_a_b], writes=[alf_b], tag=f"L{i}.f.al{n}")
                    sch.add("vector", lambda q: q.tensor_scalar(
                        nma[0:1, :], ln_be[0:1, :], -1.0, None, ALU.mult),
                        reads=[ln_be_b], writes=[nma_b], tag=f"L{i}.f.nm{n}")
                    for k in range(KT):
                        pm1, pm1b = psum[4]
                        pm2, pm2b = psum[5]
                        mm(pm1[:, :], pm1b, lgr[0:1, 128 * k:128 * (k + 1)], [lgr_b],
                           alf[0:1, :], [alf_b], True, True, f"L{i}.f.m1{k}.{n}")
                        mm(pm2[:, :], pm2b, lbr[0:1, 128 * k:128 * (k + 1)], [lbr_b],
                           onr2[0:1, :], [onr2b], True, False, f"L{i}.f.m2a{k}.{n}")
                        mm(pm2[:, :], pm2b, lgr[0:1, 128 * k:128 * (k + 1)], [lgr_b],
                           nma[0:1, :], [nma_b], False, True, f"L{i}.f.m2b{k}.{n}")
                        xt = sl[16 + k] if sub == 0 else None
                        if sub == 1:
                            xt = (stg[0][:, CH + k * CH:CH + (k + 1) * CH]
                                  if False else None)
                        if sub == 0:
                            xnt, xnb = sl[16 + k]
                            xn_ap = xnt[:, 0:CH]
                            xtr, xtw = [], []
                        else:
                            xn_ap = stg[0][:, CH * k:CH * (k + 1)]
                            xnb = vb(f"ffn.xn1.{k}")
                            frst = npair == 0 and k == 0
                            xtr = [] if frst else [stg[1]]
                            xtw = [stg[1]] if frst else []
                        sch.add("vector", lambda q, xn_ap=xn_ap, t=s2s[k][0], pm1=pm1:
                                q.tensor_tensor(xn_ap, t[:, 0:CH], pm1[:, :], ALU.mult),
                                reads=[s2s[k][1], pm1b] + xtr, writes=[xnb] + xtw,
                                tag=f"L{i}.f.xa{k}.{n}")
                        sch.add("vector", lambda q, xn_ap=xn_ap, pm2=pm2:
                                q.tensor_tensor(xn_ap, xn_ap, pm2[:, :], ALU.add),
                                reads=[xnb, pm2b], writes=[xnb], tag=f"L{i}.f.xb{k}.{n}")
                        xn_tiles[(sub, k)] = (xn_ap, xnb)
                gp_tiles = {}
                for g in range(NGT):
                    gsz = GSZ[g]
                    wa_t, wa_b, _ = load_wslice(w1, 128 * g, 128 * g + gsz,
                                                f"L{i}.f.wa{g}.{npair}")
                    wg_t, wg_b, _ = load_wslice(w1, DI + 128 * g, DI + 128 * g + gsz,
                                                f"L{i}.f.wg{g}.{npair}")
                    for sub in range(2):
                        pa, pab = psum[2 * sub]
                        pg, pgb = psum[2 * sub + 1]
                        for k in range(KT):
                            xn_ap, xnb = xn_tiles[(sub, k)]
                            mm(pa[0:gsz, :], pab, wa_t[:, gsz * k:gsz * (k + 1)], [wa_b],
                               xn_ap, [xnb], k == 0, k == KT - 1,
                               f"L{i}.f.am{g}.{npair}.{sub}.{k}")
                            mm(pg[0:gsz, :], pgb, wg_t[:, gsz * k:gsz * (k + 1)], [wg_b],
                               xn_ap, [xnb], k == 0, k == KT - 1,
                               f"L{i}.f.gm{g}.{npair}.{sub}.{k}")
                        gel = rhs[6][0][:, 512 * (g % 8):512 * (g % 8) + 512]
                        gelb = vb(f"ffn.gel{g % 8}")
                        frst = npair == 0 and g == 0 and sub == 0
                        sch.add("scalar", lambda q, gel=gel, pg=pg, gsz=gsz, g=g:
                                q.activation(gel[0:gsz, :], pg[0:gsz, :], AF.Gelu,
                                             bias=b1g_t[0:gsz, g:g + 1]),
                                reads=[pgb, b1g_b] + ([] if frst else [rhs[6][1]]),
                                writes=[gelb] + ([rhs[6][1]] if frst else []),
                                tag=f"L{i}.f.gl{g}.{npair}.{sub}")
                        base_r = 0 if sub == 0 else 3
                        rr = base_r + g // 8
                        gp_ap = rhs[rr][0][:, 512 * (g % 8):512 * (g % 8) + 512]
                        gpb = vb(f"ffn.gp{sub}.{g}")
                        frst = npair == 0 and (g % 8) == 0 and g // 8 == 0 and sub in (0, 1) and g == 0
                        grd = [] if (npair == 0 and g < 8 and (g == 0)) else [rhs[rr][1]]
                        gwr = [rhs[rr][1]] if (npair == 0 and (g % 8) == 0) else []
                        sch.add("vector", lambda q, gp_ap=gp_ap, pa=pa, gel=gel, gsz=gsz, g=g:
                                q.scalar_tensor_tensor(
                                    gp_ap[0:gsz, :], pa[0:gsz, :],
                                    b1a_t[0:gsz, g:g + 1], gel[0:gsz, :],
                                    ALU.add, ALU.mult),
                                reads=[pab, gelb, b1a_b] + grd, writes=[gpb] + gwr,
                                tag=f"L{i}.f.gp{g}.{npair}.{sub}")
                        gp_tiles[(sub, g)] = (gp_ap, gpb, gsz)
                for m in range(KT):
                    w2sl = load_w2slice(w2, m, f"L{i}.f.w2{m}.{npair}")
                    for sub in range(2):
                        n = 2 * npair + sub
                        pt, pb = psum[4 + sub]
                        for g in range(NGT):
                            wap, wbuf, gsz = w2sl[g]
                            gp_ap, gpb, _ = gp_tiles[(sub, g)]
                            mm(pt[:, :], pb, wap, [wbuf], gp_ap[0:gsz, :], [gpb],
                               g == 0, g == NGT - 1, f"L{i}.f.o{m}.{npair}.{sub}.{g}")
                        yt, yb = fsl[m % 4]
                        load(yt[:, :], yb,
                             H["xB"][m * 128:(m + 1) * 128, n * CH:(n + 1) * CH],
                             reads=[hbuf("xB", (m, n))], tag=f"L{i}.f.x{m}.{n}")
                        sch.add("vector", lambda q, yt=yt, pt=pt, m=m:
                                q.scalar_tensor_tensor(
                                    yt[:, :], pt[:, :], cols_t[:, 40 + m:41 + m],
                                    yt[:, :], ALU.add, ALU.add),
                                reads=[pb, yb, cols_b], writes=[yb],
                                tag=f"L{i}.f.y{m}.{n}")
                        store(H["xB"][m * 128:(m + 1) * 128, n * CH:(n + 1) * CH],
                              yt[:, :], yb, hbufs=[hbuf("xB", (m, n))],
                              tag=f"L{i}.f.st{m}.{n}")

        def phase_ln2():
            onc, oncb = csb["c_ones_col"]
            st_s, st_sb = psum[6]
            st_q, st_qb = psum[7]
            for n in range(NCH):
                ytiles = []
                for m in range(KT):
                    xt, xb = fsl[8]
                    load(xt[:, :], xb, H["xB"][m * 128:(m + 1) * 128, n * CH:(n + 1) * CH],
                         reads=[hbuf("xB", (m, n))], tag=f"ln2.x{m}.{n}")
                    yt, yb = fsl[m]
                    load(yt[:, :], yb,
                         H["xin2"][m * 128:(m + 1) * 128, n * CH:(n + 1) * CH],
                         reads=[hbuf("xin2", (m, n))], tag=f"ln2.x2{m}.{n}")
                    sch.add("vector", lambda q, yt=yt, xt=xt: q.tensor_tensor(
                        yt[:, :], yt[:, :], xt[:, :], ALU.add),
                        reads=[xb, yb], writes=[yb], tag=f"ln2.y{m}.{n}")
                    ytiles.append((yt, yb))
                    ybf_t, ybf_b = sl[8 + m]
                    sch.add("scalar", lambda q, ybf_t=ybf_t, yt=yt: q.activation(
                        ybf_t[:, 0:CH], yt[:, :], AF.Copy),
                        reads=[yb], writes=[ybf_b], tag=f"ln2.ybf{m}.{n}")
                    ysq_t, ysq_b = sl[16 + m % 4]
                    sch.add("scalar", lambda q, ysq_t=ysq_t, yt=yt: q.activation(
                        ysq_t[:, 0:CH], yt[:, :], AF.Square),
                        reads=[yb], writes=[ysq_b], tag=f"ln2.ysq{m}.{n}")
                    mm(st_s[0:1, :], st_sb, onc[:, :], [oncb], ybf_t[:, 0:CH], [ybf_b],
                       m == 0, m == KT - 1, f"ln2.sts{m}.{n}")
                    mm(st_q[0:1, :], st_qb, onc[:, :], [oncb], ysq_t[:, 0:CH], [ysq_b],
                       m == 0, m == KT - 1, f"ln2.stq{m}.{n}")
                sch.add("vector", lambda q: q.tensor_scalar(
                    ln_a[0:1, :], st_q[0:1, :], 1.0 / D, None, ALU.mult),
                    reads=[st_qb], writes=[ln_a_b], tag=f"ln2.v{n}")
                sch.add("vector", lambda q: q.tensor_scalar(
                    ln_mu[0:1, :], st_s[0:1, :], 1.0 / D, None, ALU.mult),
                    reads=[st_sb], writes=[ln_mu_b], tag=f"ln2.mu{n}")
                sch.add("scalar", lambda q: q.activation(
                    ln_m2[0:1, :], ln_mu[0:1, :], AF.Square),
                    reads=[ln_mu_b], writes=[ln_m2_b], tag=f"ln2.m2{n}")
                sch.add("vector", lambda q: q.tensor_tensor(
                    ln_a[0:1, :], ln_a[0:1, :], ln_m2[0:1, :], ALU.subtract),
                    reads=[ln_a_b, ln_m2_b], writes=[ln_a_b], tag=f"ln2.var{n}")
                eps_ap = fcol(LN_EPS)[0:1, :]
                sch.add("scalar", lambda q, eps_ap=eps_ap: q.activation(
                    ln_a[0:1, :], ln_a[0:1, :], AF.Ln, bias=eps_ap),
                    reads=[ln_a_b, fcol_b], writes=[ln_a_b], tag=f"ln2.ln{n}")
                nh_ap = fcol(-0.5)[0:1, :]
                sch.add("scalar", lambda q, nh_ap=nh_ap: q.activation(
                    ln_a[0:1, :], ln_a[0:1, :], AF.Exp, scale=nh_ap),
                    reads=[ln_a_b, fcol_b], writes=[ln_a_b], tag=f"ln2.ex{n}")
                sch.add("vector", lambda q: q.tensor_tensor(
                    ln_be[0:1, :], ln_mu[0:1, :], ln_a[0:1, :], ALU.mult),
                    reads=[ln_mu_b, ln_a_b], writes=[ln_be_b], tag=f"ln2.be{n}")
                pa_t, pa_b = psum[4]
                pq_t, pq_b = psum[5]
                mm(pa_t[:, :], pa_b, on1f[0:1, :], [on1fb], ln_a[0:1, :], [ln_a_b],
                   True, True, f"ln2.bca{n}")
                mm(pq_t[:, :], pq_b, on1f[0:1, :], [on1fb], ln_be[0:1, :], [ln_be_b],
                   True, True, f"ln2.bcb{n}")
                for m in range(KT):
                    yt, yb = ytiles[m]
                    sch.add("vector", lambda q, yt=yt, pa_t=pa_t: q.tensor_tensor(
                        yt[:, :], yt[:, :], pa_t[:, :], ALU.mult),
                        reads=[yb, pa_b], writes=[yb], tag=f"ln2.ap1{m}.{n}")
                    sch.add("vector", lambda q, yt=yt, pq_t=pq_t: q.tensor_tensor(
                        yt[:, :], yt[:, :], pq_t[:, :], ALU.subtract),
                        reads=[yb, pq_b], writes=[yb], tag=f"ln2.ap2{m}.{n}")
                    store(H["xB"][m * 128:(m + 1) * 128, n * CH:(n + 1) * CH], yt[:, :], yb,
                          hbufs=[hbuf("xB", (m, n))], tag=f"ln2.sx{m}.{n}")

        def phase_tout(src="xB"):
            for n in range(NCH):
                for k in range(KT):
                    t, b = fsl[k % 6]
                    load(t[:, :], b, H[src][k * 128:(k + 1) * 128, n * CH:(n + 1) * CH],
                         reads=[hbuf(src, (k, n))], tag=f"to.l{k}.{n}")
                    for c in range(4):
                        pt, pb = psum[c % 4]
                        sch.add("tensor", lambda q, pt=pt, t=t, c=c: q.transpose(
                            pt[:, 0:128], t[:, c * 128:(c + 1) * 128], idf[:, :]),
                            reads=[b, idf_b], writes=[pb], tag=f"to.t{k}.{n}.{c}")
                        et, eb = fsl[6 + c % 2]
                        sch.add("vector", lambda q, et=et, pt=pt: q.tensor_copy(
                            et[:, 0:128], pt[:, 0:128]),
                            reads=[pb], writes=[eb], tag=f"to.e{k}.{n}.{c}")
                        store(y_out[n * CH + c * 128:n * CH + (c + 1) * 128,
                                    k * 128:(k + 1) * 128],
                              et[:, 0:128], eb, tag=f"to.s{k}.{n}.{c}", out=True)

        phase_g1()
        phase_g2()
        phase_g3()
        phase_g4()
        if tap == "ln1":
            phase_tout("xB")
        else:
            nl = DEPTH_T if tap == "full" else int(tap[2:]) + 1
            for li_ in range(nl):
                phase_qkv(li_)
                phase_attn(li_)
                phase_wo(li_)
                if tap == f"at{li_}":
                    break
                phase_ffn(li_)
            if tap == "full":
                phase_ln2()
            phase_tout("xB")

        if DEBUG_DUMPS:
            for nm2, shp, dt2 in (("q", (D, S), BF16), ("k", (D, S), BF16),
                                  ("vT", (S, D), BF16), ("ao", (D, S), BF16)):
                dbg2 = nc.dram_tensor("dbg_" + nm2, list(shp), dt2,
                                      kind="ExternalOutput").ap()
                op = store(dbg2, H[nm2], Buf("dbgsrc_" + nm2), hbufs=[],
                           eng="gpsimd", tag=f"dbg.{nm2}", out=True)
                deps2 = []
                for kk, bufobj in list(gridbufs.items()) + list(viewbufs.items()):
                    nmx = kk[0] if isinstance(kk, tuple) else kk
                    if isinstance(nmx, str) and (nmx == nm2 or nmx.startswith(nm2)):
                        if bufobj.last_w is not None:
                            deps2.append(bufobj.last_w)
                op.deps.extend(deps2)
            dbgnv = nc.dram_tensor("dbg_nvq", [128, 512], F32, kind="ExternalOutput").ap()
            store(dbgnv, nvq[:, :], nvq_b, tag="dbg.nvq", out=True)
            dbg_fc = nc.dram_tensor("dbg_fcol", [128, 16], F32, kind="ExternalOutput").ap()
            store(dbg_fc, fcol_t[:, :], fcol_b, tag="dbg.fcol", out=True)
            dbg_cols = nc.dram_tensor("dbg_cols", [128, 64], F32, kind="ExternalOutput").ap()
            store(dbg_cols, cols_t[:, :], cols_b, tag="dbg.cols", out=True)
            for nm, dt in (("xh", F32), ("ga", F32), ("gb", F32), ("h", BF16),
                           ("xB", F32), ("xin2", F32), ("y1", F32)):
                dbg = nc.dram_tensor("dbg_" + nm, [D, S], dt, kind="ExternalOutput").ap()
                bb = [hbuf(nm, (k, n)) for k in range(KT) for n in range(NCH)]
                store(dbg, H[nm], Buf("dbgsrc_" + nm), hbufs=[],
                      eng="gpsimd", tag=f"dbg.{nm}", out=True)
                sch.ops["gpsimd"][-1].deps.extend(
                    [b.last_w for b in bb if b.last_w is not None])

        sch.add("sync", lambda q: None, after=out_stores, tag="final_wait")

        block = es.enter_context(nc.Block())
        sch.emit(block, sems, dsems)

    return nc


def kernel(**inputs):
    consts = _host_consts()
    prep = _host_prep(inputs)
    nc = build_program()
    base = {
        "W_dx": inputs["L0_attn_W_dx"], "W_x": inputs["L0_attn_W_x"],
        "W_gx": inputs["L0_attn_W_gx"], "W_ga": inputs["L0_attn_W_ga"],
        "W_o": inputs["L0_attn_W_o"],
        "wqkv_p": prep["wqkv_p"], "Wo_t": inputs["L0_ffn_Wo"],
        "W1": inputs["L0_ffn_W1"], "W2": inputs["L0_ffn_W2"],
        "alogit_c": prep["alogit_c"], "bx_c": prep["bx_c"],
        "bgx_c": prep["bgx_c"], "bga_c": prep["bga_c"], "bdxo": prep["bdxo"],
        "qs_c": prep["qs_c"], "qs_cs": prep["qs_cs"],
        "ks_c": prep["ks_c"], "ks_cs": prep["ks_cs"],
        "b1a_c": prep["b1a_c"], "b1g_c": prep["b1g_c"], "b2_c": prep["b2_c"],
        "lng_r": prep["lng_r"], "lnb_r": prep["lnb_r"],
    }
    base.update(consts)
    x = inputs["L0_attn_x"]
    in_maps = []
    for c in range(8):
        m = dict(base)
        m["x"] = np.ascontiguousarray(x[c])
        in_maps.append(m)
    res = run_bass_kernel_spmd(nc, in_maps, core_ids=list(range(8)))
    return np.stack([np.asarray(res.results[c]["y"]) for c in range(8)], 0)


